# version 01 (final)
# baseline (speedup 1.0000x reference)
import numpy as np
import ml_dtypes
from contextlib import ExitStack

import concourse.bass as bass
import concourse.mybir as mybir
from concourse.bass_utils import run_bass_kernel_spmd

F32 = mybir.dt.float32
BF16 = mybir.dt.bfloat16
I32 = mybir.dt.int32
AF = mybir.ActivationFunctionType
ALU = mybir.AluOpType
AX = mybir.AxisListType.X
BF_NP = np.dtype(ml_dtypes.bfloat16)

NCORE = 8
BATCH = 4
SEQ = 2048
D = 4096
NSTR = SEQ // NCORE
TOK = BATCH * NSTR
FCH = D // 128
NH = 16
HD = 256
ROT = 64
INTER = 11008
ICH = INTER // 128
TP = 6144
MASKVAL = -1.0e9

ENGINES = ("tensor", "vector", "scalar", "sync", "gpsimd")


class Sem:
    def __init__(self, nc, name):
        self._cm = nc.semaphore(name)
        self.h = None
        self.n = 0


class Prog:
    def __init__(self):
        self.streams = {e: [] for e in ENGINES}

    def op(self, eng, f, sem=None, k=1):
        if sem is None:
            self.streams[eng].append(lambda e, f=f: f(e))
            return None
        sem.n += k
        tgt = sem.n

        def runf(e, f=f, h=sem, kk=k):
            f(e).then_inc(h.h, kk)
        self.streams[eng].append(runf)
        return tgt

    def wait(self, eng, sem, tgt):
        if tgt is None or tgt <= 0:
            return
        self.streams[eng].append(lambda e, s=sem, t=tgt: e.wait_ge(s.h, t))


def _ceil(a, b):
    return -(-a // b)


def build_program(trunc=None):
    nc = bass.Bass("TRN2", target_bir_lowering=False)

    def ein(name, shape, dt):
        return nc.dram_tensor(name, list(shape), dt, kind="ExternalInput").ap()

    x0f = ein("x0f", (D, TOK), F32)
    wqT = ein("wqT", (D, D), F32)
    wkT = ein("wkT", (D, D), F32)
    wvT = ein("wvT", (D, D), F32)
    woT = ein("woT", (D, D), F32)
    wqkvT = ein("wqkvT", (D, TP), F32)
    wprT = ein("wprT", (D, D), F32)
    qw_in = {"g": ein("qwg", (D // 8, INTER), I32),
             "u": ein("qwu", (D // 8, INTER), I32),
             "d": ein("qwd", (INTER // 8, D), I32)}
    sc_in = {"g": ein("scg", (32, INTER), F32),
             "u": ein("scu", (32, INTER), F32),
             "d": ein("scd", (86, D), F32)}
    qz_in = {"g": ein("qzg", (32, INTER // 8), I32),
             "u": ein("qzu", (32, INTER // 8), I32),
             "d": ein("qzd", (86, D // 8), I32)}
    cosT = ein("cosT", (64, TOK), BF16)
    sinT = ein("sinT", (64, TOK), BF16)
    maskA = ein("maskA", (SEQ, NSTR), BF16)
    identb = ein("identb", (128, 128), BF16)
    identf = ein("identf", (128, 128), F32)
    onescb = ein("onescb", (128, 1), BF16)
    onesrf = ein("onesrf", (1, 128), F32)

    outf = nc.dram_tensor("outf", [D, TOK], F32, kind="ExternalOutput").ap()

    def idram(name, shape, dt):
        return nc.dram_tensor(name, list(shape), dt, kind="Internal").ap()

    wdq = {"g": idram("wdg", (D, INTER), BF16),
           "u": idram("wdu", (D, INTER), BF16),
           "d": idram("wdd", (INTER, D), BF16)}
    zD = {"g": idram("zgD", (32, INTER), I32),
          "u": idram("zuD", (32, INTER), I32),
          "d": idram("zdD", (86, D), I32)}
    kagi = idram("kagi", (D, TOK), BF16)
    vagi = idram("vagi", (TOK, D), BF16)
    kago = idram("kago", (NCORE * D, TOK), BF16)
    vago = idram("vago", (NCORE * TOK, D), BF16)
    xsp = {"x1": idram("x1sp", (D, TOK), F32),
           "x2": idram("x2sp", (D, TOK), F32),
           "x3": idram("x3sp", (D, TOK), F32)}

    p = Prog()
    ctx = ExitStack()
    with ctx:
        E = ctx.enter_context
        A = E(nc.sbuf_tensor("A", [128, 32768], BF16))
        Bb = E(nc.sbuf_tensor("Bb", [128, 57344], BF16))
        Cb = E(nc.sbuf_tensor("Cb", [128, 5632], BF16))
        cos_sb = E(nc.sbuf_tensor("cos_sb", [64, TOK], BF16))
        sin_sb = E(nc.sbuf_tensor("sin_sb", [64, TOK], BF16))
        identb_sb = E(nc.sbuf_tensor("identb_sb", [128, 128], BF16))
        onescb_sb = E(nc.sbuf_tensor("onescb_sb", [128, 1], BF16))
        onesrf_sb = E(nc.sbuf_tensor("onesrf_sb", [1, 128], F32))
        mean_sb = E(nc.sbuf_tensor("mean_sb", [1, TOK], F32))
        rstd_sb = E(nc.sbuf_tensor("rstd_sb", [1, TOK], F32))
        trow1 = E(nc.sbuf_tensor("trow1", [1, 512], F32))
        trow2 = E(nc.sbuf_tensor("trow2", [1, 512], F32))
        rs_sb = E(nc.sbuf_tensor("rs_sb", [1, 256], F32))
        bc_sb = E(nc.sbuf_tensor("bc_sb", [128, 256], F32))
        smax_sb = E(nc.sbuf_tensor("smax_sb", [128, 32], F32))
        sexp_sb = E(nc.sbuf_tensor("sexp_sb", [128, 128], F32))
        ssum_sb = E(nc.sbuf_tensor("ssum_sb", [128, 32], F32))

        Bap = Bb.ap()
        Bf = Bap.bitcast(F32)
        Bi = Bap.bitcast(I32)
        Cap = Cb.ap()
        Cf = Cap.bitcast(F32)

        psA = [E(nc.psum_tensor(f"psA{i}", [128, 512], F32)) for i in range(4)]
        psB = [E(nc.psum_tensor(f"psB{i}", [128, 512], F32)) for i in range(2)]
        psC = E(nc.psum_tensor("psC", [128, 512], F32))
        psR = E(nc.psum_tensor("psR", [1, 512], F32))

        sems = {}
        for nm in ("qS", "qG", "PE", "DV", "ACp", "CC", "ST", "WL0", "WL1", "SG0", "SG1"):
            s = Sem(nc, nm)
            s.h = E(s._cm)
            sems[nm] = s
        qS, qG, PE, DV, ACp, CC, ST = (sems[k] for k in ("qS", "qG", "PE", "DV", "ACp", "CC", "ST"))
        WL = [sems["WL0"], sems["WL1"]]
        SG = [sems["SG0"], sems["SG1"]]

        def dma(eng, out, in_, sem=None):
            sem = sem or (qG if eng == "gpsimd" else qS)
            return p.op(eng, lambda e, o=out, i=in_: e.dma_start(out=o, in_=i), sem, 16)

        def barrier():
            for eng in ENGINES:
                for s in (PE, DV, ACp, qS, qG, WL[0], WL[1], SG[0], SG[1]):
                    p.wait(eng, s, s.n)

        dma("sync", cos_sb[:], cosT[:])
        dma("sync", sin_sb[:], sinT[:])
        dma("sync", identb_sb[:], identb[:])
        dma("sync", onescb_sb[:], onescb[:])
        t_const = dma("sync", onesrf_sb[:], onesrf[:])
        for eng in ("tensor", "vector", "scalar", "gpsimd"):
            p.wait(eng, qS, t_const)

        xb_st = Cap[:, 4096:4608]
        x2b_st = Cap[:, 4608:5120]
        sguard = {"pe": 0}

        def ln_stats_chunk(xc, half, first, last, dv_tgt=None):
            p.wait("vector", PE, sguard["pe"])
            p.wait("scalar", PE, sguard["pe"])
            if dv_tgt is not None:
                p.wait("scalar", DV, dv_tgt)
            tD = p.op("vector", lambda v, x=xc: v.tensor_copy(out=xb_st, in_=x), DV)
            tA = p.op("scalar", lambda s, x=xc: s.activation(out=x2b_st, in_=x, func=AF.Square), ACp)
            p.wait("tensor", DV, tD)
            p.wait("tensor", ACp, tA)
            ps_s = psR[0:1, 0:512] if half == 0 else psB[0][0:1, 0:512]
            ps_q = psC[0:1, 0:512] if half == 0 else psB[1][0:1, 0:512]
            p.op("tensor", lambda t, o=ps_s, st=first, sp=last:
                 t.matmul(o, onescb_sb[:], xb_st, start=st, stop=sp, skip_group_check=True))
            tP = p.op("tensor", lambda t, o=ps_q, st=first, sp=last:
                      t.matmul(o, onescb_sb[:], x2b_st, start=st, stop=sp, skip_group_check=True), PE)
            sguard["pe"] = tP
            return tP

        def ln_finish_half(half):
            ps_s = psR[0:1, 0:512] if half == 0 else psB[0][0:1, 0:512]
            ps_q = psC[0:1, 0:512] if half == 0 else psB[1][0:1, 0:512]
            mcol = mean_sb[0:1, half * 512:(half + 1) * 512]
            rcol = rstd_sb[0:1, half * 512:(half + 1) * 512]
            p.op("vector", lambda v, i=ps_s, o=mcol: v.tensor_scalar_mul(o, i, 1.0 / D))
            p.op("vector", lambda v, i=ps_q: v.tensor_scalar_mul(trow1[:], i, 1.0 / D))
            p.op("vector", lambda v, m=mcol: v.tensor_mul(trow2[:], m, m))
            tD = p.op("vector", lambda v: v.scalar_tensor_tensor(
                trow1[:], trow1[:], 1e-5, trow2[:], op0=ALU.add, op1=ALU.subtract), DV)
            p.wait("scalar", DV, tD)
            tA = p.op("scalar", lambda s: s.activation(out=trow2[:], in_=trow1[:],
                                                       func=AF.Sqrt), ACp)
            p.wait("vector", ACp, tA)
            tD = p.op("vector", lambda v, o=rcol: v.reciprocal(o, trow2[:]), DV)
            return tD

        def ln_broadcast(tgt_dv):
            p.wait("tensor", DV, tgt_dv)
            p.wait("tensor", DV, DV.n)
            tP = None
            for half in range(2):
                sl = slice(half * 512, (half + 1) * 512)
                p.op("tensor", lambda t, o=psA[half][:, 0:512], r=mean_sb[0:1, sl]:
                     t.matmul(o, onesrf_sb[:], r, start=True, stop=True, skip_group_check=True))
                tP = p.op("tensor", lambda t, o=psA[2 + half][:, 0:512], r=rstd_sb[0:1, sl]:
                          t.matmul(o, onesrf_sb[:], r, start=True, stop=True,
                                   skip_group_check=True), PE)
            return tP

        def ln_normalize_stream(src_dram, tP_bc):
            first = True
            for fc in range(FCH):
                for half in range(2):
                    par = half
                    xc = Cf[:, par * 512:(par + 1) * 512]
                    p.wait("sync", DV, DV.n)
                    tL = dma("sync", xc, src_dram[fc * 128:(fc + 1) * 128,
                                                  half * 512:(half + 1) * 512])
                    p.wait("vector", qS, tL)
                    if first:
                        p.wait("vector", PE, tP_bc)
                        first = False
                    tmp = Cf[:, 1024 + par * 512:1536 + par * 512]
                    ysl = A[:, fc * 1024 + half * 512:fc * 1024 + (half + 1) * 512]
                    p.op("vector", lambda v, x=xc, m=psA[half][:, 0:512], o=tmp:
                         v.tensor_sub(o, x, m))
                    p.op("vector", lambda v, t2=tmp, r=psA[2 + half][:, 0:512], o=ysl:
                         v.tensor_mul(o, t2, r), DV)

        for half in range(2):
            for fc in range(FCH):
                par = fc % 2
                xc = Cf[:, par * 512:(par + 1) * 512]
                p.wait("sync", DV, DV.n)
                p.wait("sync", ACp, ACp.n)
                tL = dma("sync", xc, x0f[fc * 128:(fc + 1) * 128, half * 512:(half + 1) * 512])
                p.wait("vector", qS, tL)
                p.wait("scalar", qS, tL)
                ln_stats_chunk(xc, half, first=(fc == 0), last=(fc == FCH - 1))
            p.wait("vector", PE, PE.n)
            tD_fin = ln_finish_half(half)
        tP_bc = ln_broadcast(tD_fin)
        ln_normalize_stream(x0f, tP_bc)

        def proj_w_stationary(wT_dram, evac, wbuf_off):
            p.wait("gpsimd", PE, PE.n)
            p.wait("gpsimd", DV, DV.n)
            wbufs = [Bap[:, wbuf_off:wbuf_off + 4096],
                     Bap[:, wbuf_off + 4096:wbuf_off + 8192]]
            tload = [0, 0]
            guard = {}
            for oc in range(FCH):
                par = oc % 2
                if oc >= 2:
                    p.wait("gpsimd", PE, PE.n)
                tload[par] = dma("gpsimd", wbufs[par],
                                 wT_dram.rearrange("(fc pp) o -> pp fc o", pp=128)[:, :, oc * 128:(oc + 1) * 128],
                                 sem=WL[par])
                p.wait("tensor", WL[par], tload[par])
                for half in range(2):
                    psi = par * 2 + half
                    ps = psA[psi]
                    g = guard.get(psi)
                    if g is not None:
                        p.wait("tensor", g[0], g[1])
                    else:
                        p.wait("tensor", DV, DV.n)
                    for fc in range(FCH):
                        p.op("tensor", lambda t, o=ps[:, 0:512],
                             l=wbufs[par][:, fc * 128:(fc + 1) * 128],
                             r=A[:, fc * 1024 + half * 512:fc * 1024 + (half + 1) * 512],
                             st=(fc == 0), sp=(fc == FCH - 1):
                             t.matmul(o, l, r, start=st, stop=sp, skip_group_check=True),
                             PE if fc == FCH - 1 else None)
                    guard[psi] = evac(oc, half, ps, PE.n)

        def proj_y_stationary(wT_dram, n_out, ocw, evac, wbuf_off, nbuf):
            p.wait("gpsimd", PE, PE.n)
            p.wait("gpsimd", DV, DV.n)
            wbw = FCH * ocw
            wbufs = [Bap[:, wbuf_off + i * wbw:wbuf_off + (i + 1) * wbw] for i in range(nbuf)]
            guard = {}
            it = 0
            for ocb in range(n_out // ocw):
                par = ocb % nbuf
                if ocb >= nbuf:
                    p.wait("gpsimd", PE, PE.n)
                tload = dma("gpsimd", wbufs[par],
                            wT_dram.rearrange("(fc pp) o -> pp fc o", pp=128)[:, :, ocb * ocw:(ocb + 1) * ocw],
                            sem=WL[par])
                p.wait("tensor", WL[par], tload)
                for tc in range(8):
                    psi = it % 4
                    it += 1
                    ps = psA[psi]
                    g = guard.get(psi)
                    if g is not None:
                        p.wait("tensor", g[0], g[1])
                    else:
                        p.wait("tensor", DV, DV.n)
                    for fc in range(FCH):
                        p.op("tensor", lambda t, o=ps[:, 0:ocw],
                             l=A[:, fc * 1024 + tc * 128:fc * 1024 + (tc + 1) * 128],
                             r=wbufs[par][:, fc * ocw:(fc + 1) * ocw],
                             st=(fc == 0), sp=(fc == FCH - 1):
                             t.matmul(o, l, r, start=st, stop=sp, skip_group_check=True),
                             PE if fc == FCH - 1 else None)
                    guard[psi] = evac(ocb, tc, ps, PE.n)

        def evac_rope(ps, mm_tgt, oc, half, dst_col0, dst_buf):
            p.wait("vector", PE, mm_tgt)
            dsl = slice(dst_col0 + half * 512, dst_col0 + (half + 1) * 512)
            csl = slice(half * 512, (half + 1) * 512)
            if oc % 2 == 0:
                t1 = Cf[:, 1024:1536]
                t2 = Cf[:, 1536:2048]
                p.op("vector", lambda v, i=ps[0:32, 0:512], c=cos_sb[0:32, csl], o=t1[0:32, :]:
                     v.tensor_mul(o, i, c))
                p.op("vector", lambda v, i=ps[32:64, 0:512], s2=sin_sb[32:64, csl], o=t2[0:32, :]:
                     v.tensor_mul(o, i, s2))
                p.op("vector", lambda v, o=dst_buf[0:32, dsl]:
                     v.tensor_sub(o, t1[0:32, :], t2[0:32, :]))
                p.op("vector", lambda v, i=ps[32:64, 0:512], c=cos_sb[32:64, csl], o=t1[32:64, :]:
                     v.tensor_mul(o, i, c))
                p.op("vector", lambda v, i=ps[0:32, 0:512], s2=sin_sb[0:32, csl], o=t2[32:64, :]:
                     v.tensor_mul(o, i, s2))
                p.op("vector", lambda v, o=dst_buf[32:64, dsl]:
                     v.tensor_add(o, t1[32:64, :], t2[32:64, :]))
                tD = p.op("vector", lambda v, i=ps[64:128, 0:512], o=dst_buf[64:128, dsl]:
                          v.tensor_copy(out=o, in_=i), DV)
            else:
                tD = p.op("vector", lambda v, i=ps[:, 0:512], o=dst_buf[:, dsl]:
                          v.tensor_copy(out=o, in_=i), DV)
            return tD

        KSTG = 32768
        WBOF = 34816
        stg_guard = [0, 0]

        def evac_k(oc, half, ps, mm_tgt):
            spar = oc % 2
            stile = Bap[:, KSTG + spar * 1024:KSTG + (spar + 1) * 1024]
            if half == 0:
                p.wait("vector", SG[spar], stg_guard[spar])
            tD = evac_rope(ps, mm_tgt, oc, half, KSTG + spar * 1024, Bb)
            if half == 1:
                p.wait("sync", DV, tD)
                stg_guard[spar] = dma("sync", kagi[oc * 128:(oc + 1) * 128, :], stile,
                                      sem=SG[spar])
            return (DV, tD)

        proj_w_stationary(wkT, evac_k, WBOF)
        p.wait("gpsimd", qS, qS.n)
        p.wait("gpsimd", SG[0], SG[0].n)
        p.wait("gpsimd", SG[1], SG[1].n)
        tCC_k = p.op("gpsimd", lambda g: g.collective_compute(
            "AllGather", ALU.bypass, replica_groups=[list(range(NCORE))],
            ins=[kagi.opt()], outs=[kago.opt()]), CC)

        stg_guard = [SG[0].n, SG[1].n]

        def evac_v(ocb, tc, ps, mm_tgt):
            spar = (ocb * 8 + tc) % 2
            stile = Bap[:, KSTG + spar * 1024:KSTG + spar * 1024 + 256]
            p.wait("vector", SG[spar], stg_guard[spar])
            p.wait("vector", PE, mm_tgt)
            tD = p.op("vector", lambda v, i=ps[:, 0:256], o=stile: v.tensor_copy(out=o, in_=i), DV)
            p.wait("sync", DV, tD)
            stg_guard[spar] = dma("sync", vagi[tc * 128:(tc + 1) * 128,
                                               ocb * 256:(ocb + 1) * 256], stile, sem=SG[spar])
            return (DV, tD)

        proj_y_stationary(wvT, D, 256, evac_v, WBOF, 2)
        p.wait("gpsimd", qS, qS.n)
        p.wait("gpsimd", SG[0], SG[0].n)
        p.wait("gpsimd", SG[1], SG[1].n)
        tCC_v = p.op("gpsimd", lambda g: g.collective_compute(
            "AllGather", ALU.bypass, replica_groups=[list(range(NCORE))],
            ins=[vagi.opt()], outs=[vago.opt()]), CC)

        def evac_q(oc, half, ps, mm_tgt):
            tD = evac_rope(ps, mm_tgt, oc, half, oc * 1024, Bb)
            return (DV, tD)

        proj_w_stationary(wqT, evac_q, WBOF)

        if trunc in ("kag", "vag", "q"):
            barrier()
            p.wait("sync", CC, CC.n)
            if trunc == "kag":
                tO = dma("gpsimd", outf.rearrange("d t -> (d t)"),
                         kago[0:D, :].rearrange("d t -> (d t)"), sem=ST)
            elif trunc == "vag":
                tO = dma("gpsimd", outf.rearrange("d t -> (d t)"),
                         vago[0:TOK, :].rearrange("d t -> (d t)"), sem=ST)
            else:
                tO = dma("gpsimd", outf.rearrange("(oc pp) t -> pp oc t", pp=128),
                         Bap[:, 0:32768].rearrange("pp (oc t) -> pp oc t", t=TOK), sem=ST)
            p.wait("sync", ST, tO)
            _finalize(p, nc)
            return nc

        barrier()
        p.wait("sync", CC, tCC_k)
        p.wait("sync", CC, tCC_v)
        KOFF = 32768
        VOFF = 36864
        MOFF = 40960
        PTO = 0
        Msb = Bap[:, MOFF:MOFF + 4096]
        t_mask = dma("sync", Msb.rearrange("p (kt q) -> p kt q", q=256),
                     maskA.rearrange("(kt p) q -> p kt q", p=128))
        p.wait("tensor", qS, t_mask)
        kv_guard = PE.n
        for b in range(BATCH):
            for h in range(NH):
                p.wait("sync", PE, kv_guard)
                for dc in range(2):
                    src = kago.rearrange("(r d) t -> d r t", r=NCORE)[
                        (2 * h + dc) * 128:(2 * h + dc + 1) * 128, :, b * 256:(b + 1) * 256]
                    dst = Bap[:, KOFF + dc * 2048:KOFF + (dc + 1) * 2048].rearrange(
                        "pp (r t) -> pp r t", r=NCORE)
                    dma("sync", dst, src)
                tKV = None
                for hf in range(2):
                    vsrc = vago.rearrange("(r hh pp) d -> pp r hh d", r=NCORE, hh=8)[
                        :, :, 2 * b + hf, h * 256:(h + 1) * 256]
                    vdst = Bap[:, VOFF:VOFF + 4096].rearrange(
                        "pp (r hh d) -> pp r hh d", r=NCORE, hh=2)[:, :, hf, :]
                    tKV = dma("sync", vdst, vsrc)
                p.wait("tensor", qS, tKV)
                p.wait("tensor", DV, DV.n)
                for kt in range(16):
                    psi = kt % 4
                    ps = psA[psi]
                    for dc in range(2):
                        p.op("tensor", lambda t, o=ps[:, 0:256],
                             l=Bap[:, KOFF + dc * 2048 + kt * 128:KOFF + dc * 2048 + (kt + 1) * 128],
                             r=Bap[:, (2 * h + dc) * 1024 + b * 256:(2 * h + dc) * 1024 + (b + 1) * 256],
                             st=(dc == 0):
                             t.matmul(o, l, r, start=st, stop=False, skip_group_check=True))
                    tP = p.op("tensor", lambda t, o=ps[:, 0:256],
                              r=Msb[:, kt * 256:(kt + 1) * 256]:
                              t.matmul(o, identb_sb[:], r, start=False, stop=True,
                                       skip_group_check=True), PE)
                    p.wait("scalar", PE, tP)
                    ptile = Cap[:, PTO + psi * 256:PTO + (psi + 1) * 256]
                    tE = p.op("scalar", lambda s, i=ps[:, 0:256], o=ptile:
                              s.activation(out=o, in_=i, func=AF.Exp, scale=0.0625), ACp)
                    p.wait("tensor", ACp, tE)
                    p.op("tensor", lambda t, pt=ptile, st=(kt == 0), sp=(kt == 15):
                         t.matmul(psR[0:1, 0:256], onescb_sb[:], pt, start=st, stop=sp,
                                  skip_group_check=True))
                    for dc in range(2):
                        p.op("tensor", lambda t, o=psB[dc][:, 0:256],
                             l=Bap[:, VOFF + kt * 256 + dc * 128:VOFF + kt * 256 + (dc + 1) * 128],
                             pt=ptile, st=(kt == 0), sp=(kt == 15):
                             t.matmul(o, l, pt, start=st, stop=sp, skip_group_check=True),
                             PE if (kt == 15 and dc == 1) else None)
                tPav = PE.n
                p.wait("vector", PE, tPav)
                tD = p.op("vector", lambda v: v.reciprocal(rs_sb[:], psR[0:1, 0:256]), DV)
                p.wait("tensor", DV, tD)
                tP = p.op("tensor", lambda t: t.matmul(psC[:, 0:256], onesrf_sb[:], rs_sb[:],
                                                       start=True, stop=True,
                                                       skip_group_check=True), PE)
                p.wait("vector", PE, tP)
                p.op("vector", lambda v: v.tensor_copy(out=bc_sb[:], in_=psC[:, 0:256]))
                for dc in range(2):
                    osl = A[:, (2 * h + dc) * 1024 + b * 256:(2 * h + dc) * 1024 + (b + 1) * 256]
                    p.op("vector", lambda v, i=psB[dc][:, 0:256], o=osl:
                         v.tensor_mul(o, i, bc_sb[:]), DV if dc == 1 else None)
                p.wait("tensor", DV, DV.n)
                kv_guard = PE.n
                if trunc == "p0" and b == 0 and h == 0:
                    barrier()
                    dma("gpsimd", outf[0:128, 0:256],
                        Cap[:, PTO + 0 * 256:PTO + 1 * 256], sem=ST)
                    dma("sync", outf[128:129, 0:256], rs_sb[:], sem=ST)
                    dma("sync", outf[256:384, 0:256], bc_sb[:], sem=ST)
                    tO = dma("gpsimd", outf[512:640, 0:256],
                             A.ap()[:, 0 * 1024 + 0 * 256:0 * 1024 + 1 * 256], sem=ST)
                    p.wait("sync", ST, tO)
                    _finalize(p, nc)
                    return nc

        if trunc == "att":
            barrier()
            tO = dma("gpsimd", outf.rearrange("(ec pp) t -> pp ec t", pp=128),
                     A.ap().rearrange("pp (ec t) -> pp ec t", t=TOK), sem=ST)
            p.wait("sync", ST, tO)
            _finalize(p, nc)
            return nc

        def make_resid_evac(base_dram, spill_dram, do_stats):
            cnt = {"n": 0}

            def evac(oc, half, ps, mm_tgt):
                par = cnt["n"] % 2
                cnt["n"] += 1
                xc = Cf[:, par * 512:(par + 1) * 512]
                p.wait("sync", DV, DV.n)
                tL = dma("sync", xc, base_dram[oc * 128:(oc + 1) * 128,
                                               half * 512:(half + 1) * 512])
                p.wait("vector", qS, tL)
                p.wait("vector", PE, mm_tgt)
                xo = Cf[:, 1024 + par * 512:1536 + par * 512]
                tD = p.op("vector", lambda v, i=ps[:, 0:512], x=xc, o=xo:
                          v.tensor_add(o, i, x), DV)
                p.wait("sync", DV, tD)
                dma("sync", spill_dram[oc * 128:(oc + 1) * 128,
                                       half * 512:(half + 1) * 512], xo)
                if do_stats:
                    ln_stats_chunk(xo, half, first=(oc == 0), last=(oc == FCH - 1),
                                   dv_tgt=tD)
                return (DV, tD)
            return evac

        barrier()
        proj_w_stationary(woT, make_resid_evac(x0f, xsp["x1"], True), 0)
        for half in range(2):
            p.wait("vector", PE, PE.n)
            tD_fin = ln_finish_half(half)

        if trunc == "x1":
            p.wait("sync", DV, DV.n)
            p.wait("sync", qS, qS.n)
            tO = dma("sync", outf[:, :], xsp["x1"][:, :], sem=ST)
            p.wait("sync", ST, tO)
            _finalize(p, nc)
            return nc

        tP_bc = ln_broadcast(tD_fin)
        ln_normalize_stream(xsp["x1"], tP_bc)

        barrier()

        def dequant_matrix(m, K_, N_, G_):
            qz_sb = Bi[0:G_, 0:N_ // 8]
            zfull = Bi[0:G_, 2048:2048 + N_]
            tL = dma("sync", qz_sb, qz_in[m][:, :])
            p.wait("vector", qS, tL)
            for j in range(8):
                zv = Bi[0:G_, 2048:2048 + N_].rearrange("g (c e) -> g c e", e=8)[:, :, j:j + 1]
                p.op("vector", lambda v, o=zv, i=qz_sb.unsqueeze(2), sh=4 * j:
                     v.tensor_scalar(out=o, in0=i, scalar1=sh, scalar2=15,
                                     op0=ALU.arith_shift_right, op1=ALU.bitwise_and))
            tD = p.op("vector", lambda v: v.tensor_scalar_add(zfull, zfull, 1), DV)
            p.wait("sync", DV, tD)
            tZ = dma("sync", zD[m][:, :], zfull)
            p.wait("sync", qS, tZ)

            KW = K_ // 8
            rblocks = _ceil(KW, 128)
            oc_list = []
            o0 = 0
            while o0 < N_:
                oc_list.append((o0, min(512, N_ - o0)))
                o0 += 512
            qwb = [Bi[:, 13312:13824], Bi[:, 13824:14336]]
            zrb = [Bi[:, 14336:14848], Bi[:, 14848:15360]]
            srb = [Bap[:, 31744:32256], Bap[:, 32256:32768]]
            w4b = Bi[:, 15360:15872]
            wzb = Bap[:, 33792:34304]
            wob = [Bap[:, 34304:34816], Bap[:, 34816:35328]]
            it = 0
            for rb in range(rblocks):
                pr = min(128, KW - rb * 128)
                for (o0, ow) in oc_list:
                    par = it % 2
                    it += 1
                    p.wait("sync", DV, DV.n)
                    p.wait("gpsimd", DV, DV.n)
                    dma("sync", qwb[par][0:pr, 0:ow], qw_in[m][rb * 128:rb * 128 + pr, o0:o0 + ow])
                    g0 = rb * 8
                    gn = _ceil(pr, 16)
                    zsrc = zD[m].unsqueeze(1).broadcast_to([G_, 16, N_])[g0:g0 + gn, :, o0:o0 + ow]
                    tZr = dma("sync", zrb[par][0:pr, 0:ow], zsrc)
                    ssrc = sc_in[m].unsqueeze(1).broadcast_to([G_, 16, N_])[g0:g0 + gn, :, o0:o0 + ow]
                    tS = dma("gpsimd", srb[par][0:pr, 0:ow], ssrc)
                    p.wait("vector", qS, tZr)
                    p.wait("vector", qG, tS)
                    for j in range(8):
                        wpar = j % 2
                        p.wait("vector", qS, qS.n)
                        p.op("vector", lambda v, i=qwb[par][0:pr, 0:ow], sh=4 * j,
                             o=w4b[0:pr, 0:ow]:
                             v.tensor_scalar(out=o, in0=i, scalar1=sh, scalar2=15,
                                             op0=ALU.arith_shift_right, op1=ALU.bitwise_and))
                        p.op("vector", lambda v, a=w4b[0:pr, 0:ow], z=zrb[par][0:pr, 0:ow],
                             o=wzb[0:pr, 0:ow]: v.tensor_sub(o, a, z))
                        tD = p.op("vector", lambda v, a=wzb[0:pr, 0:ow], s=srb[par][0:pr, 0:ow],
                                  o=wob[wpar][0:pr, 0:ow]: v.tensor_mul(o, a, s), DV)
                        p.wait("sync", DV, tD)
                        dst = wdq[m].rearrange("(r j) n -> j r n", j=8)[
                            j, rb * 128:rb * 128 + pr, o0:o0 + ow]
                        dma("sync", dst, wob[wpar][0:pr, 0:ow])

        dequant_matrix("g", D, INTER, 32)
        dequant_matrix("u", D, INTER, 32)
        dequant_matrix("d", INTER, D, 86)

        def ffn_pass(base_dram, spill_dram, do_stats, final_out):
            barrier()
            HOFF = 0
            WG0 = 44032
            WU0 = 48128
            WD0 = 44032
            hg = Bap[:, 52224:52736]
            tD_f = None
            for tc in range(2):
                for ic in range(ICH):
                    par = ic % 2
                    p.wait("sync", PE, PE.n)
                    dma("sync", Bap[:, WG0:WG0 + 4096],
                        wdq["g"].rearrange("(fc pp) n -> pp fc n", pp=128)[:, :, ic * 128:(ic + 1) * 128])
                    tW = dma("sync", Bap[:, WU0:WU0 + 4096],
                             wdq["u"].rearrange("(fc pp) n -> pp fc n", pp=128)[:, :, ic * 128:(ic + 1) * 128])
                    p.wait("tensor", qS, tW)
                    p.wait("tensor", DV, DV.n)
                    for w0, ps in ((WG0, psA[par]), (WU0, psA[2 + par])):
                        for fc in range(FCH):
                            p.op("tensor", lambda t, o=ps[:, 0:512],
                                 l=Bap[:, w0 + fc * 128:w0 + (fc + 1) * 128],
                                 r=A[:, fc * 1024 + tc * 512:fc * 1024 + (tc + 1) * 512],
                                 st=(fc == 0), sp=(fc == FCH - 1):
                                 t.matmul(o, l, r, start=st, stop=sp, skip_group_check=True),
                                 PE if fc == FCH - 1 else None)
                    tmm = PE.n
                    p.wait("scalar", PE, tmm)
                    p.wait("scalar", DV, DV.n)
                    tA = p.op("scalar", lambda s, i=psA[par][:, 0:512]:
                              s.activation(out=hg, in_=i, func=AF.Silu), ACp)
                    p.wait("vector", ACp, tA)
                    p.op("vector", lambda v, u=psA[2 + par][:, 0:512],
                         o=Bap[:, HOFF + ic * 512:HOFF + (ic + 1) * 512]:
                         v.tensor_mul(o, hg, u), DV)
                p.wait("sync", DV, DV.n)
                p.wait("tensor", DV, DV.n)
                for hc in range(FCH):
                    par = hc % 2
                    p.wait("sync", PE, PE.n)
                    tW = dma("sync", Bap[:, WD0:WD0 + 11008],
                             wdq["d"].rearrange("(ic pp) n -> pp ic n", pp=128)[:, :, hc * 128:(hc + 1) * 128])
                    p.wait("tensor", qS, tW)
                    p.wait("tensor", DV, DV.n)
                    ps = psA[par]
                    for ic in range(ICH):
                        p.op("tensor", lambda t, o=ps[:, 0:512],
                             l=Bap[:, WD0 + ic * 128:WD0 + (ic + 1) * 128],
                             r=Bap[:, HOFF + ic * 512:HOFF + (ic + 1) * 512],
                             st=(ic == 0), sp=(ic == ICH - 1):
                             t.matmul(o, l, r, start=st, stop=sp, skip_group_check=True),
                             PE if ic == ICH - 1 else None)
                    tmm = PE.n
                    sp_par = hc % 2
                    xc = Cf[:, sp_par * 512:(sp_par + 1) * 512]
                    p.wait("sync", DV, DV.n)
                    tL = dma("sync", xc, base_dram[hc * 128:(hc + 1) * 128,
                                                   tc * 512:(tc + 1) * 512])
                    p.wait("vector", qS, tL)
                    p.wait("vector", PE, tmm)
                    xo = Cf[:, 1024 + sp_par * 512:1536 + sp_par * 512]
                    tD = p.op("vector", lambda v, i=ps[:, 0:512], x=xc, o=xo:
                              v.tensor_add(o, i, x), DV)
                    p.wait("sync", DV, tD)
                    dst = (outf if final_out else spill_dram)
                    dma("sync", dst[hc * 128:(hc + 1) * 128, tc * 512:(tc + 1) * 512], xo,
                        sem=(ST if final_out else qS))
                    if do_stats:
                        ln_stats_chunk(xo, tc, first=(hc == 0), last=(hc == FCH - 1),
                                       dv_tgt=tD)
                if do_stats:
                    p.wait("vector", PE, PE.n)
                    tD_f = ln_finish_half(tc)
            return tD_f

        tD_fin = ffn_pass(xsp["x1"], xsp["x2"], True, False)

        if trunc == "x2":
            p.wait("sync", DV, DV.n)
            p.wait("sync", qS, qS.n)
            tO = dma("sync", outf[:, :], xsp["x2"][:, :], sem=ST)
            p.wait("sync", ST, tO)
            _finalize(p, nc)
            return nc

        tP_bc = ln_broadcast(tD_fin)
        ln_normalize_stream(xsp["x2"], tP_bc)

        if trunc == "y3":
            barrier()
            tO = dma("gpsimd", outf.rearrange("(ec pp) t -> pp ec t", pp=128),
                     A.ap().rearrange("pp (ec t) -> pp ec t", t=TOK), sem=ST)
            p.wait("sync", ST, tO)
            _finalize(p, nc)
            return nc
        if trunc == "ln3":
            barrier()
            dma("sync", outf[0:1, :], mean_sb[:], sem=ST)
            tO = dma("sync", outf[1:2, :], rstd_sb[:], sem=ST)
            p.wait("sync", ST, tO)
            _finalize(p, nc)
            return nc

        barrier()
        QTM, KTM, VTM = 0, 32768, 40960
        WQB = 49152

        def evac_qkv(ocb, tc, ps, mm_tgt):
            p.wait("vector", PE, mm_tgt)
            base = ocb * 256
            if base < D:
                dst = Bap[:, QTM + tc * 4096 + base:QTM + tc * 4096 + base + 256]
            elif base < D + 1024:
                o = base - D
                dst = Bap[:, KTM + tc * 1024 + o:KTM + tc * 1024 + o + 256]
            else:
                o = base - D - 1024
                dst = Bap[:, VTM + tc * 1024 + o:VTM + tc * 1024 + o + 256]
            tD = p.op("vector", lambda v, i=ps[:, 0:256], o2=dst: v.tensor_copy(out=o2, in_=i), DV)
            return (DV, tD)

        proj_y_stationary(wqkvT, TP, 256, evac_qkv, WQB, 1)

        if trunc == "qkvb":
            barrier()
            tO = dma("gpsimd", outf.rearrange("(a pp) t -> pp a t", pp=128),
                     Bap[:, 0:32768].rearrange("pp (a t) -> pp a t", t=1024), sem=ST)
            p.wait("sync", ST, tO)
            _finalize(p, nc)
            return nc

        barrier()
        ACC0 = 24576
        identf_C = Cf[:, 2304:2432]
        t_idf = dma("sync", identf_C, identf[:])
        p.wait("tensor", qS, t_idf)
        scale_b = float(1.0 / np.sqrt(128.0))
        tr_guard = {}

        for half in range(2):
            for s in range(BATCH):
                tc_s = s * 2 + half
                p.wait("vector", ACp, ACp.n)
                p.wait("vector", PE, PE.n)
                scr = Cf[:, 0:128]
                tD = None
                for t_ in range(s + 1):
                    tc_t = t_ * 2 + half
                    for hq in range(32):
                        qj = Bap[:, QTM + tc_s * 4096 + hq * 128:QTM + tc_s * 4096 + (hq + 1) * 128]
                        kj = Bap[:, KTM + tc_t * 1024 + (hq // 4) * 128:KTM + tc_t * 1024 + (hq // 4 + 1) * 128]
                        last = (t_ == s and hq == 31)
                        tD = p.op("vector", lambda v, q=qj, k=kj,
                                  a=sexp_sb[:, t_ * 32 + hq:t_ * 32 + hq + 1]:
                                  v.scalar_tensor_tensor(scr, q, 1.0, k, op0=ALU.mult,
                                                         op1=ALU.mult, accum_out=a),
                                  DV if last else None)
                p.wait("vector", DV, tD)
                if trunc == "pb" and half == 0 and s == 1:
                    tmpX = Cf[:, 1024:2048]
                    tX4 = tmpX.rearrange("pp (hh m d) -> pp hh m d", hh=2, m=4)
                    qX = Bap[:, QTM + 2 * 4096:QTM + 2 * 4096 + 1024].rearrange(
                        "pp (hh m d) -> pp hh m d", hh=2, m=4)
                    kX = Bap[:, KTM:KTM + 256].rearrange(
                        "pp (hh d) -> pp hh d", hh=2).unsqueeze(2).broadcast_to([128, 2, 4, 128])
                    tM = p.op("vector", lambda v: v.tensor_mul(tX4, qX, kX), DV)
                    p.wait("vector", DV, tM)
                    tD = p.op("vector", lambda v: v.tensor_reduce(
                        ssum_sb[:, 0:8], tmpX.rearrange("pp (h d) -> pp h d", d=128),
                        axis=AX, op=ALU.add), DV)
                    p.wait("sync", DV, tD)
                    dma("sync", outf[0:128, 0:128], sexp_sb[:], sem=ST)
                    dma("sync", outf[384:392, 0:8].rearrange("a b -> b a"),
                        ssum_sb[:, 0:8], sem=ST) if False else None
                    dma("sync", outf[1024:1152, 0:8], ssum_sb[:, 0:8], sem=ST)
                    dma("sync", outf[1152:1280, 0:1024], tmpX, sem=ST)
                    dma("gpsimd", outf[640:768, 0:1024],
                        Bap[:, QTM + 2 * 4096:QTM + 2 * 4096 + 1024], sem=ST)
                    dma("gpsimd", outf[768:896, 0:256],
                        Bap[:, KTM:KTM + 256], sem=ST)
                    p.wait("vector", ST, ST.n)
                    p.wait("scalar", ST, ST.n)
                tD = p.op("vector", lambda v: v.tensor_copy(out=smax_sb[:], in_=sexp_sb[:, 0:32]),
                          DV)
                for t_ in range(1, s + 1):
                    tD = p.op("vector", lambda v, i=sexp_sb[:, t_ * 32:(t_ + 1) * 32]:
                              v.tensor_max(smax_sb[:], smax_sb[:], i), DV)
                p.wait("vector", DV, tD)
                if trunc == "pb" and half == 0 and s == 1:
                    tD = p.op("vector", lambda v: v.engine_nop(), DV)
                    p.wait("sync", DV, tD)
                    dma("sync", outf[384:512, 0:32], smax_sb[:], sem=ST)
                    p.wait("vector", ST, ST.n)
                    p.wait("scalar", ST, ST.n)
                tD = None
                for t_ in range(s + 1):
                    tD = p.op("vector", lambda v, i=sexp_sb[:, t_ * 32:(t_ + 1) * 32]:
                              v.tensor_sub(i, i, smax_sb[:]), DV)
                p.wait("scalar", DV, tD)
                tA = p.op("scalar", lambda s_, n=s + 1: s_.activation(
                    out=sexp_sb[:, 0:n * 32], in_=sexp_sb[:, 0:n * 32],
                    func=AF.Exp, scale=scale_b), ACp)
                p.wait("vector", ACp, tA)
                tD = p.op("vector", lambda v: v.tensor_copy(out=ssum_sb[:], in_=sexp_sb[:, 0:32]),
                          DV)
                for t_ in range(1, s + 1):
                    tD = p.op("vector", lambda v, i=sexp_sb[:, t_ * 32:(t_ + 1) * 32]:
                              v.tensor_add(ssum_sb[:], ssum_sb[:], i), DV)
                p.wait("vector", DV, tD)
                tD = p.op("vector", lambda v: v.reciprocal(ssum_sb[:], ssum_sb[:]), DV)
                p.wait("vector", DV, tD)
                tD = None
                for t_ in range(s + 1):
                    tD = p.op("vector", lambda v, i=sexp_sb[:, t_ * 32:(t_ + 1) * 32]:
                              v.tensor_mul(i, i, ssum_sb[:]), DV)
                p.wait("vector", DV, tD)
                if trunc == "pb" and half == 0 and s == 1:
                    tD = p.op("vector", lambda v: v.engine_nop(), DV)
                    p.wait("sync", DV, tD)
                    dma("sync", outf[128:256, 0:128], sexp_sb[:], sem=ST)
                    p.wait("vector", ST, ST.n)
                    p.wait("scalar", ST, ST.n)
                acc = Bf[:, ACC0:ACC0 + 4096]
                for t_ in range(s + 1):
                    tc_t = t_ * 2 + half
                    for hg in range(4):
                        tmp = Cf[:, 0:1024]
                        tmp4 = tmp.rearrange("pp (hh m d) -> pp hh m d", hh=2, m=4)
                        a_x = sexp_sb[:, t_ * 32 + hg * 8:t_ * 32 + (hg + 1) * 8]
                        a4 = a_x.rearrange("pp (hh m) -> pp hh m", hh=2).unsqueeze(3).broadcast_to([128, 2, 4, 128])
                        vv = Bap[:, VTM + tc_t * 1024 + hg * 256:VTM + tc_t * 1024 + (hg + 1) * 256]
                        v4 = vv.rearrange("pp (hh d) -> pp hh d", hh=2).unsqueeze(2).broadcast_to([128, 2, 4, 128])
                        tD = p.op("vector", lambda v, a=a4, w=v4, o=tmp4:
                                  v.tensor_mul(o, a, w), DV)
                        p.wait("vector", DV, tD)
                        osl = acc[:, hg * 1024:(hg + 1) * 1024]
                        if t_ == 0:
                            tD = p.op("vector", lambda v, i=tmp, o=osl:
                                      v.tensor_copy(out=o, in_=i), DV)
                        else:
                            tD = p.op("vector", lambda v, i=tmp, o=osl:
                                      v.tensor_add(o, o, i), DV)
                        p.wait("vector", DV, tD)
                accbf = Bap[:, QTM + tc_s * 4096:QTM + (tc_s + 1) * 4096]
                tD = p.op("vector", lambda v, o=accbf: v.tensor_copy(out=o, in_=acc), DV)
                if trunc == "pb" and half == 0 and s == 1:
                    p.wait("sync", DV, tD)
                    dma("sync", outf[2048:2176, 0:1024], acc[:, 0:1024], sem=ST)
                    dma("sync", outf[2432:2560, 0:1024], acc[:, 1024:2048], sem=ST)
                    p.wait("vector", ST, ST.n)
                    p.wait("tensor", ST, ST.n)
                p.wait("tensor", DV, tD)
                for ec in range(FCH):
                    psi = ec % 4
                    ps = psA[psi]
                    g = tr_guard.get(psi)
                    if g is not None:
                        p.wait("tensor", DV, g)
                    tP = p.op("tensor", lambda t, l=accbf[:, ec * 128:(ec + 1) * 128],
                              o=ps[:, 0:128]: t.matmul(o, l, identb_sb[:], start=True,
                                                       stop=True, skip_group_check=True), PE)
                    p.wait("vector", PE, tP)
                    osl = A[:, ec * 1024 + tc_s * 128:ec * 1024 + (tc_s + 1) * 128]
                    tr_guard[psi] = p.op("vector", lambda v, i=ps[:, 0:128], o=osl:
                                         v.tensor_copy(out=o, in_=i), DV)
                if trunc == "pb" and half == 0 and s == 1:
                    p.wait("gpsimd", DV, DV.n)
                    p.wait("gpsimd", PE, PE.n)
                    dma("gpsimd", outf[2176:2304, 0:1024],
                        A.ap()[:, 0:1024], sem=ST)
                    tO = dma("gpsimd", outf[2304:2432, 0:1024],
                             Bap[:, QTM + 2 * 4096:QTM + 2 * 4096 + 1024], sem=ST)
                    p.wait("sync", ST, tO)
                    _finalize(p, nc)
                    return nc

        if trunc == "attb":
            barrier()
            tO = dma("gpsimd", outf.rearrange("(ec pp) t -> pp ec t", pp=128),
                     A.ap().rearrange("pp (ec t) -> pp ec t", t=TOK), sem=ST)
            p.wait("sync", ST, tO)
            _finalize(p, nc)
            return nc

        barrier()
        proj_w_stationary(wprT, make_resid_evac(xsp["x2"], xsp["x3"], True), 0)
        for half in range(2):
            p.wait("vector", PE, PE.n)
            tD_fin = ln_finish_half(half)

        if trunc == "x3":
            p.wait("sync", DV, DV.n)
            p.wait("sync", qS, qS.n)
            tO = dma("sync", outf[:, :], xsp["x3"][:, :], sem=ST)
            p.wait("sync", ST, tO)
            _finalize(p, nc)
            return nc

        tP_bc = ln_broadcast(tD_fin)
        ln_normalize_stream(xsp["x3"], tP_bc)

        ffn_pass(xsp["x3"], None, False, True)
        p.wait("sync", ST, ST.n)
        _finalize(p, nc)
    return nc


def _finalize(p, nc):
    with nc.Block() as block:
        @block.tensor
        def _(t):
            for f in p.streams["tensor"]:
                f(t)

        @block.vector
        def _(v):
            for f in p.streams["vector"]:
                f(v)

        @block.scalar
        def _(s):
            for f in p.streams["scalar"]:
                f(s)

        @block.sync
        def _(sy):
            for f in p.streams["sync"]:
                f(sy)

        @block.gpsimd
        def _(g):
            for f in p.streams["gpsimd"]:
                f(g)



def _rope_perm():
    perm = []
    for h in range(NH):
        base = h * HD
        perm += [base + 2 * i for i in range(32)]
        perm += [base + 2 * i + 1 for i in range(32)]
        perm += list(range(base + 64, base + 128))
        perm += list(range(base + 128, base + 256))
    return np.array(perm, dtype=np.int64)


def _qkv_perm():
    q, k, v = [], [], []
    for hk in range(8):
        for m in range(4):
            q += list(range(hk * 768 + m * 128, hk * 768 + (m + 1) * 128))
        k += list(range(hk * 768 + 512, hk * 768 + 640))
        v += list(range(hk * 768 + 640, hk * 768 + 768))
    return np.array(q + k + v, dtype=np.int64)


def _host_stage(inputs):
    x0 = np.ascontiguousarray(inputs["L0_attn_a_hidden_states"])
    rp = _rope_perm()
    qp = _qkv_perm()
    shared = {
        "wqT": np.ascontiguousarray(inputs["L0_attn_a_w_q"].T[:, rp]),
        "wkT": np.ascontiguousarray(inputs["L0_attn_a_w_k"].T[:, rp]),
        "wvT": np.ascontiguousarray(inputs["L0_attn_a_w_v"].T),
        "woT": np.ascontiguousarray(inputs["L0_attn_a_w_out"].T),
        "wqkvT": np.ascontiguousarray(inputs["L0_attn_b_weight_qkv"].T[:, qp]),
        "wprT": np.ascontiguousarray(inputs["L0_attn_b_weight_proj"].T),
        "qwg": inputs["L0_ffn_gate_qweight"], "qwu": inputs["L0_ffn_up_qweight"],
        "qwd": inputs["L0_ffn_down_qweight"],
        "scg": inputs["L0_ffn_gate_scales"], "scu": inputs["L0_ffn_up_scales"],
        "scd": inputs["L0_ffn_down_scales"],
        "qzg": inputs["L0_ffn_gate_qzeros"], "qzu": inputs["L0_ffn_up_qzeros"],
        "qzd": inputs["L0_ffn_down_qzeros"],
    }
    ident = np.eye(128, dtype=np.float32)
    shared["identb"] = ident.astype(BF_NP)
    shared["identf"] = ident
    shared["onescb"] = np.ones((128, 1), dtype=np.float32).astype(BF_NP)
    shared["onesrf"] = np.ones((1, 128), dtype=np.float32)

    inv_freq = 1.0 / 10000.0 ** (np.arange(0, ROT, 2, dtype=np.float64) / ROT)
    in_maps = []
    for c in range(NCORE):
        sl = slice(c * NSTR, (c + 1) * NSTR)
        x0c = np.ascontiguousarray(x0[:, sl, :].reshape(TOK, D).T)
        npos = np.arange(c * NSTR, (c + 1) * NSTR, dtype=np.float64)
        ang = npos[None, :] * inv_freq[:, None]
        cos1 = np.tile(np.cos(ang), (1, BATCH)).astype(np.float32)
        sin1 = np.tile(np.sin(ang), (1, BATCH)).astype(np.float32)
        cosc = np.ascontiguousarray(np.concatenate([cos1, cos1], axis=0).astype(BF_NP))
        sinc = np.ascontiguousarray(np.concatenate([sin1, sin1], axis=0).astype(BF_NP))
        kt = np.arange(SEQ, dtype=np.int64)[:, None]
        qq = (c * NSTR + np.arange(NSTR, dtype=np.int64))[None, :]
        mask = np.where(kt <= qq, 0.0, MASKVAL).astype(np.float32).astype(BF_NP)
        m = dict(shared)
        m.update({"x0f": x0c, "cosT": cosc, "sinT": sinc, "maskA": np.ascontiguousarray(mask)})
        in_maps.append(m)
    return in_maps


def kernel(**inputs):
    in_maps = _host_stage(inputs)
    nc = build_program(trunc=None)
    res = run_bass_kernel_spmd(nc, in_maps, core_ids=list(range(NCORE)))
    out = np.empty((BATCH, SEQ, D), dtype=np.float32)
    for c in range(NCORE):
        shard = res.results[c]["outf"]
        out[:, c * NSTR:(c + 1) * NSTR, :] = shard.T.reshape(BATCH, NSTR, D)
    return out
